# version 03
# baseline (speedup 1.0000x reference)
import numpy as np
from contextlib import ExitStack

import concourse.bass as bass
import concourse.mybir as mybir
from concourse.bass_utils import run_bass_kernel_spmd

F32 = mybir.dt.float32
BF16 = mybir.dt.bfloat16
I32 = mybir.dt.int32
AF = mybir.ActivationFunctionType
ALU = mybir.AluOpType
AX = mybir.AxisListType

T = 2048
D = 4096
DFF = 11008
NH = 32
NT = T // 128
NDC = D // 128
NFC = DFF // 128
NQ = T // 512
C_SCALE = float(1.0 / np.sqrt(128.0))
EXP_BIAS = -12.0
LN_EPS = 1e-5
INV_D = 1.0 / D
PARTS = [(0, 29), (29, 29), (58, 28)]


def seg_list(qi):
    kend = (qi + 1) * 128
    if kend <= 1024:
        return [(0, kend)]
    return [(0, 1024), (1024, kend - 1024)]


def n_tgroups(qi):
    return -(-(qi + 1) // 4)


class Sem:
    def __init__(self, nc, e, name):
        self.h = e(nc.semaphore(name))
        self.n = 0

    def tick(self, ins, k=1):
        ins.then_inc(self.h, k)
        self.n += k
        return self.n


def build_nc():
    nc = bass.Bass("TRN2", target_bir_lowering=False)

    x_d = nc.dram_tensor("x", [T, D], F32, kind="ExternalInput").ap()
    wq_d = nc.dram_tensor("wq_p", [NH, 128, D], F32, kind="ExternalInput").ap()
    wk_d = nc.dram_tensor("wk_p", [128, D], F32, kind="ExternalInput").ap()
    wv_d = nc.dram_tensor("wv_p", [128, D], F32, kind="ExternalInput").ap()
    wo_d = nc.dram_tensor("wo_p", [NDC, 128, D], F32, kind="ExternalInput").ap()
    gw_d = nc.dram_tensor("gw_p", [NFC, 128, D], I32, kind="ExternalInput").ap()
    uw_d = nc.dram_tensor("uw_p", [NFC, 128, D], I32, kind="ExternalInput").ap()
    dw_d = nc.dram_tensor("dw_p", [NDC, 128, DFF], I32, kind="ExternalInput").ap()
    gs_d = nc.dram_tensor("gs_p", [128, NFC], F32, kind="ExternalInput").ap()
    us_d = nc.dram_tensor("us_p", [128, NFC], F32, kind="ExternalInput").ap()
    ds_d = nc.dram_tensor("ds_p", [128, NDC], F32, kind="ExternalInput").ap()
    out_d = nc.dram_tensor("out", [T, D], F32, kind="ExternalOutput").ap()

    xn_dram = nc.dram_tensor("xn_dram", [T, D], BF16, kind="Internal").ap()
    oT_dram = nc.dram_tensor("oT_dram", [NH, 128, T], BF16, kind="Internal").ap()
    h_dram = nc.dram_tensor("h_dram", [DFF, T], BF16, kind="Internal").ap()
    attnT_dram = nc.dram_tensor("attnT_dram", [D, T], BF16, kind="Internal").ap()
    downT_dram = nc.dram_tensor("downT_dram", [D, T], BF16, kind="Internal").ap()
    dpart_dram = nc.dram_tensor("dpart_dram", [D, T], F32, kind="Internal").ap()

    with ExitStack() as ctx:
        e = ctx.enter_context

        cst = e(nc.sbuf_tensor("cst", [128, 8], F32))
        maskA = e(nc.sbuf_tensor("maskA", [128, 128], F32))
        ident = e(nc.sbuf_tensor("ident", [128, 128], BF16))
        gs_s = e(nc.sbuf_tensor("gs_s", [128, NFC], F32))
        us_s = e(nc.sbuf_tensor("us_s", [128, NFC], F32))
        ds_s = e(nc.sbuf_tensor("ds_s", [128, NDC], F32))
        kT = e(nc.sbuf_tensor("kT", [128, T], BF16))
        vtok = e(nc.sbuf_tensor("vtok", [128, T], BF16))
        sums0 = e(nc.sbuf_tensor("sums0", [128, 4 * NT], F32))
        sums1 = e(nc.sbuf_tensor("sums1", [128, 4 * NT], F32))
        stat = e(nc.sbuf_tensor("stat", [128, 16], F32))

        s_pe = Sem(nc, e, "s_pe")
        s_act = Sem(nc, e, "s_act")
        s_dve = Sem(nc, e, "s_dve")
        s_pool = Sem(nc, e, "s_pool")
        s_dp = Sem(nc, e, "s_dp")
        s_ds = Sem(nc, e, "s_ds")
        s_dc = Sem(nc, e, "s_dc")
        s_te = Sem(nc, e, "s_te")

        block = e(nc.Block())

        consts = [EXP_BIAS, C_SCALE, 0.0, 1.0, LN_EPS, -0.5]
        for i, cv in enumerate(consts):
            nc.const_aps.register(F32, cv, cst[:, i:i + 1])

        @block.gpsimd
        def _(g):
            for i, cv in enumerate(consts):
                g.memset(cst[:, i:i + 1], cv)
            g.memset(maskA[:], 0.0)
            g.memset(ident[:], 1.0)
            g.affine_select(out=maskA[:], in_=maskA[:], pattern=[[-1, 128]],
                            compare_op=ALU.is_ge, fill=-1e30, base=0,
                            channel_multiplier=1)
            ins = g.affine_select(out=ident[:], in_=ident[:], pattern=[[1, 128]],
                                  compare_op=ALU.is_equal, fill=0.0, base=0,
                                  channel_multiplier=-1)
            s_pool.tick(ins)
            for src, dst in ((gs_d, gs_s), (us_d, us_s), (ds_d, ds_s)):
                g.dma_start(out=dst[:], in_=src).then_inc(s_dp.h, 16)
                s_dp.n += 16
        pool_setup = s_pool.n

        sc1 = nc.enter_named_scope('p1_ln', False)
        p1_cms = [nc.sbuf_tensor("xt0", [128, D], F32),
                  nc.sbuf_tensor("xt1", [128, D], F32),
                  nc.sbuf_tensor("xnb0", [128, D], BF16),
                  nc.sbuf_tensor("xnb1", [128, D], BF16),
                  nc.sbuf_tensor("sqscr", [128, D], BF16)]
        xt0, xt1, xnb0, xnb1, sqscr = [cm.__enter__() for cm in p1_cms]
        xt = [xt0, xt1]
        xnb = [xnb0, xnb1]

        x_load = {}
        xn_store = {}
        var_tick = {}
        ln_done = {}

        @block.sync
        def _(s):
            for i in range(2):
                s.dma_start(out=xt[i][:], in_=x_d[i * 128:(i + 1) * 128, :]).then_inc(s_ds.h, 16)
                s_ds.n += 16
                x_load[i] = s_ds.n

        for i in range(NT):
            buf = xt[i % 2]
            ob = xnb[i % 2]
            c0 = (i % 2) * 8

            @block.vector
            def _(v, i=i, buf=buf, c0=c0):
                if i == 0:
                    v.wait_ge(s_pool.h, pool_setup)
                v.wait_ge(s_ds.h, x_load[i])
                ins = v.reduce_sum(stat[:, c0:c0 + 1], buf[:], axis=AX.X)
                s_dve.tick(ins)
                v.wait_ge(s_dve.h, s_dve.n)
                ins = v.tensor_scalar(out=stat[:, c0 + 1:c0 + 2], in0=stat[:, c0:c0 + 1],
                                      scalar1=-INV_D, scalar2=None, op0=ALU.mult)
                s_dve.tick(ins)
                v.wait_ge(s_dve.h, s_dve.n)
                ins = v.tensor_tensor(out=stat[:, c0 + 2:c0 + 3], in0=stat[:, c0 + 1:c0 + 2],
                                      in1=stat[:, c0 + 1:c0 + 2], op=ALU.mult)
                s_dve.tick(ins)
            nm_tick = s_dve.n

            @block.scalar
            def _(sc, i=i, buf=buf, c0=c0):
                sc.wait_ge(s_ds.h, x_load[i])
                sc.wait_ge(s_pool.h, pool_setup)
                if i >= 2:
                    sc.wait_ge(s_dve.h, var_tick[i - 2])
                ins = sc.activation(out=sqscr[:], in_=buf[:], func=AF.Square,
                                    bias=0.0, scale=1.0,
                                    accum_out=stat[:, c0 + 3:c0 + 4])
                s_act.tick(ins)
            s2_tick = s_act.n

            @block.vector
            def _(v, i=i, c0=c0):
                v.wait_ge(s_act.h, s2_tick)
                v.wait_ge(s_dve.h, nm_tick)
                ins = v.tensor_scalar(out=stat[:, c0 + 4:c0 + 5], in0=stat[:, c0 + 3:c0 + 4],
                                      scalar1=INV_D, scalar2=stat[:, c0 + 2:c0 + 3],
                                      op0=ALU.mult, op1=ALU.subtract)
                s_dve.tick(ins)
                var_tick[i] = s_dve.n

            @block.scalar
            def _(sc, i=i, c0=c0):
                sc.wait_ge(s_dve.h, var_tick[i])
                ins = sc.activation(out=stat[:, c0 + 5:c0 + 6], in_=stat[:, c0 + 4:c0 + 5],
                                    func=AF.Ln, bias=LN_EPS, scale=1.0)
                s_act.tick(ins)
                sc.wait_ge(s_act.h, s_act.n)
                ins = sc.activation(out=stat[:, c0 + 6:c0 + 7], in_=stat[:, c0 + 5:c0 + 6],
                                    func=AF.Exp, bias=0.0, scale=-0.5)
                s_act.tick(ins)
            rstd_tick = s_act.n

            @block.vector
            def _(v, i=i, buf=buf, ob=ob, c0=c0):
                v.wait_ge(s_act.h, rstd_tick)
                if i >= 2:
                    v.wait_ge(s_ds.h, xn_store[i - 2])
                ins = v.tensor_scalar(out=ob[:], in0=buf[:],
                                      scalar1=stat[:, c0 + 1:c0 + 2],
                                      scalar2=stat[:, c0 + 6:c0 + 7],
                                      op0=ALU.add, op1=ALU.mult)
                s_dve.tick(ins)
                ln_done[i] = s_dve.n

            @block.sync
            def _(s, i=i, ob=ob):
                s.wait_ge(s_dve.h, ln_done[i])
                s.dma_start(out=xn_dram[i * 128:(i + 1) * 128, :], in_=ob[:]).then_inc(s_ds.h, 16)
                s_ds.n += 16
                xn_store[i] = s_ds.n
                if i + 2 < NT:
                    s.dma_start(out=xt[i % 2][:],
                                in_=x_d[(i + 2) * 128:(i + 3) * 128, :]).then_inc(s_ds.h, 16)
                    s_ds.n += 16
                    x_load[i + 2] = s_ds.n

        for cm in reversed(p1_cms):
            cm.__exit__(None, None, None)
        xnT_cm = nc.sbuf_tensor("xnT", [128, NDC * T], BF16)
        xnT = xnT_cm.__enter__()

        @block.sync
        def _(s):
            s.wait_ge(s_ds.h, xn_store[NT - 1])
            s.wait_ge(s_dve.h, ln_done[NT - 1])
            for dc in range(NDC):
                s.dma_start_transpose(xnT[:, dc * T:(dc + 1) * T],
                                      xn_dram[:, dc * 128:(dc + 1) * 128]).then_inc(s_ds.h, 16)
                s_ds.n += 16
        xnT_all = s_ds.n

        nc.leave_named_scope('p1_ln', sc1[0], False)
        sc2 = nc.enter_named_scope('p2_kv', False)
        p2_cms = [nc.sbuf_tensor("wkv_b", [128, 2 * D], BF16),
                  nc.sbuf_tensor("vT_tmp", [128, T], BF16)]
        wkv_b, vT_tmp = [cm.__enter__() for cm in p2_cms]

        ps_big = e(nc.psum_tensor("ps_big", [128, 2048], F32))
        pt_cms = [nc.psum_tensor("ps_t0", [128, 1024], BF16),
                  nc.psum_tensor("ps_t1", [128, 1024], BF16),
                  nc.psum_tensor("psO", [128, 512], F32)]
        ps_t0, ps_t1, psO = [cm.__enter__() for cm in pt_cms]

        @block.gpsimd
        def _(g):
            g.dma_start(out=wkv_b[:, 0:D], in_=wk_d).then_inc(s_dp.h, 16)
            g.dma_start(out=wkv_b[:, D:2 * D], in_=wv_d).then_inc(s_dp.h, 16)
            s_dp.n += 32
        wkv_loaded = s_dp.n

        kv_mm = {}
        kv_evac = {}
        for which in range(2):
            @block.tensor
            def _(t, which=which):
                if which == 0:
                    t.wait_ge(s_pool.h, pool_setup)
                    t.wait_ge(s_dp.h, wkv_loaded)
                    t.wait_ge(s_ds.h, xnT_all)
                else:
                    t.wait_ge(s_dve.h, kv_evac[0])
                for dc in range(NDC):
                    for m in range(4):
                        ins = t.matmul(
                            ps_big[:, m * 512:(m + 1) * 512],
                            wkv_b[:, which * D + dc * 128:which * D + (dc + 1) * 128],
                            xnT[:, dc * T + m * 512:dc * T + (m + 1) * 512],
                            start=(dc == 0), stop=(dc == NDC - 1))
                s_pe.tick(ins)
                kv_mm[which] = s_pe.n

            @block.vector
            def _(v, which=which):
                v.wait_ge(s_pe.h, kv_mm[which])
                dst = kT if which == 0 else vT_tmp
                ins = v.tensor_copy(dst[:], ps_big[:])
                s_dve.tick(ins)
                kv_evac[which] = s_dve.n

        vtr = {}
        vte = {}
        for kb in range(NT):
            @block.tensor
            def _(t, kb=kb):
                if kb == 0:
                    t.wait_ge(s_dve.h, kv_evac[1])
                if kb >= 2:
                    t.wait_ge(s_dve.h, vte[kb - 2])
                pst = ps_t0 if kb % 2 == 0 else ps_t1
                ins = t.transpose(pst[:, 0:128], vT_tmp[:, kb * 128:(kb + 1) * 128], ident[:])
                s_pe.tick(ins)
                vtr[kb] = s_pe.n

            @block.vector
            def _(v, kb=kb):
                v.wait_ge(s_pe.h, vtr[kb])
                pst = ps_t0 if kb % 2 == 0 else ps_t1
                ins = v.tensor_copy(vtok[:, kb * 128:(kb + 1) * 128], pst[:, 0:128])
                s_dve.tick(ins)
                vte[kb] = s_dve.n
        vtok_done = s_dve.n

        nc.leave_named_scope('p2_kv', sc2[0], False)
        sc3 = nc.enter_named_scope('p3_attn', False)
        for cm in reversed(p2_cms):
            cm.__exit__(None, None, None)
        p3_cms = [nc.sbuf_tensor("wq_b0", [128, D], BF16),
                  nc.sbuf_tensor("wq_b1", [128, D], BF16),
                  nc.sbuf_tensor("qT_h", [128, T], BF16),
                  nc.sbuf_tensor("Pb0", [128, T], BF16),
                  nc.sbuf_tensor("Pb1", [128, T], BF16),
                  nc.sbuf_tensor("PT", [128, NT * 512], BF16),
                  nc.sbuf_tensor("oT_strip0", [128, T], BF16),
                  nc.sbuf_tensor("oT_strip1", [128, T], BF16)]
        wq_b0, wq_b1, qT_h, Pb0, Pb1, PT, oT_s0, oT_s1 = [cm.__enter__() for cm in p3_cms]
        wq_b = [wq_b0, wq_b1]
        Pb = [Pb0, Pb1]
        oT_strip = [oT_s0, oT_s1]

        segs = []
        for qi in range(NT):
            for j, (st0, ln) in enumerate(seg_list(qi)):
                segs.append((qi, j, st0, ln))
        NSEG = len(segs)
        si_of = {}
        last_seg_idx = {}
        for si, (qi, j, *_r) in enumerate(segs):
            si_of[(qi, j)] = si
            last_seg_idx[qi] = si

        wq_dma = {}
        after_qmm = {}
        after_qevac = {}
        after_scores = {}
        after_exp = {}
        after_mask = {}
        after_pscale = {}
        after_tgroup_g = {}
        after_pv = {}
        after_psoevac = {}
        oT_store = {}
        P_last_reader = {0: 0, 1: 0}
        gcounter = [0]

        @block.gpsimd
        def _(g):
            for hh in range(2):
                g.dma_start(out=wq_b[hh][:], in_=wq_d[hh]).then_inc(s_dp.h, 16)
                s_dp.n += 16
                wq_dma[hh] = s_dp.n

        for h in range(NH):
            sums = sums0 if h % 2 == 0 else sums1
            ostrip = oT_strip[h % 2]

            @block.tensor
            def _(t, h=h):
                t.wait_ge(s_dp.h, wq_dma[h])
                if h == 0:
                    t.wait_ge(s_dve.h, vtok_done)
                else:
                    t.wait_ge(s_act.h, after_exp[(h - 1, NSEG - 1)])
                for dc in range(NDC):
                    for m in range(4):
                        ins = t.matmul(
                            ps_big[:, m * 512:(m + 1) * 512],
                            wq_b[h % 2][:, dc * 128:(dc + 1) * 128],
                            xnT[:, dc * T + m * 512:dc * T + (m + 1) * 512],
                            start=(dc == 0), stop=(dc == NDC - 1))
                s_pe.tick(ins)
                after_qmm[h] = s_pe.n

            @block.vector
            def _(v, h=h):
                v.wait_ge(s_pe.h, after_qmm[h])
                ins = v.tensor_copy(qT_h[:], ps_big[:])
                s_dve.tick(ins)
                after_qevac[h] = s_dve.n

            if h + 2 < NH:
                @block.gpsimd
                def _(g, h=h):
                    g.wait_ge(s_pe.h, after_qmm[h])
                    g.dma_start(out=wq_b[h % 2][:], in_=wq_d[h + 2]).then_inc(s_dp.h, 16)
                    s_dp.n += 16
                    wq_dma[h + 2] = s_dp.n

            def emit_scores(qi, h=h):
                @block.tensor
                def _(t, qi=qi, h=h):
                    for j, (st0, ln) in enumerate(seg_list(qi)):
                        si = si_of[(qi, j)]
                        half = si % 2
                        if si == 0:
                            t.wait_ge(s_dve.h, after_qevac[h])
                        if si >= 2:
                            t.wait_ge(s_act.h, after_exp[(h, si - 2)])
                        for m in range(-(-ln // 512)):
                            o0 = m * 512
                            w = min(512, ln - o0)
                            ins = t.matmul(
                                ps_big[:, half * 1024 + o0:half * 1024 + o0 + w],
                                qT_h[:, qi * 128:(qi + 1) * 128],
                                kT[:, st0 + o0:st0 + o0 + w],
                                start=True, stop=True)
                        s_pe.tick(ins)
                        after_scores[(h, si)] = s_pe.n

            def emit_softmax(qi, h=h, sums=sums):
                @block.vector
                def _(v, qi=qi, h=h):
                    si = last_seg_idx[qi]
                    _q, j, st0, ln = segs[si]
                    half = si % 2
                    dcol = qi * 128 - st0
                    v.wait_ge(s_pe.h, after_scores[(h, si)])
                    ins = v.tensor_tensor(
                        out=ps_big[:, half * 1024 + dcol:half * 1024 + dcol + 128],
                        in0=ps_big[:, half * 1024 + dcol:half * 1024 + dcol + 128],
                        in1=maskA[:], op=ALU.add)
                    s_dve.tick(ins)
                    after_mask[(h, qi)] = s_dve.n

                @block.scalar
                def _(sc, qi=qi, h=h, sums=sums):
                    for j, (st0, ln) in enumerate(seg_list(qi)):
                        si = si_of[(qi, j)]
                        half = si % 2
                        if si == last_seg_idx[qi]:
                            sc.wait_ge(s_dve.h, after_mask[(h, qi)])
                        else:
                            sc.wait_ge(s_pe.h, after_scores[(h, si)])
                        sc.wait_ge(s_pe.h, P_last_reader[qi % 2])
                        ins = sc.activation(
                            out=Pb[qi % 2][:, st0:st0 + ln],
                            in_=ps_big[:, half * 1024:half * 1024 + ln],
                            func=AF.Exp, bias=EXP_BIAS, scale=C_SCALE,
                            accum_out=sums[:, 4 * qi + j:4 * qi + j + 1])
                        s_act.tick(ins)
                        after_exp[(h, si)] = s_act.n

                @block.vector
                def _(v, qi=qi, h=h, sums=sums):
                    si = last_seg_idx[qi]
                    v.wait_ge(s_act.h, after_exp[(h, si)])
                    if len(seg_list(qi)) == 2:
                        ins = v.tensor_tensor(out=sums[:, 4 * qi + 2:4 * qi + 3],
                                              in0=sums[:, 4 * qi:4 * qi + 1],
                                              in1=sums[:, 4 * qi + 1:4 * qi + 2],
                                              op=ALU.add)
                        s_dve.tick(ins)
                        v.wait_ge(s_dve.h, s_dve.n)
                        src = sums[:, 4 * qi + 2:4 * qi + 3]
                    else:
                        src = sums[:, 4 * qi:4 * qi + 1]
                    ins = v.reciprocal(sums[:, 4 * qi + 3:4 * qi + 4], src)
                    s_dve.tick(ins)
                    v.wait_ge(s_dve.h, s_dve.n)
                    kend = (qi + 1) * 128
                    ins = v.tensor_scalar(
                        out=Pb[qi % 2][:, 0:kend], in0=Pb[qi % 2][:, 0:kend],
                        scalar1=sums[:, 4 * qi + 3:4 * qi + 4], scalar2=None,
                        op0=ALU.mult)
                    s_dve.tick(ins)
                    after_pscale[(h, qi)] = s_dve.n

            def emit_transposes(qi, h=h):
                ng = n_tgroups(qi)
                base_g = gcounter[0]
                gcounter[0] += ng

                @block.tensor
                def _(t, qi=qi, h=h, base_g=base_g, ng=ng):
                    t.wait_ge(s_dve.h, after_pscale[(h, qi)])
                    for gidx in range(ng):
                        gabs = base_g + gidx
                        if gabs >= 2:
                            t.wait_ge(s_te.h, gabs - 1)
                        pst = ps_t0 if gabs % 2 == 0 else ps_t1
                        for kb in range(gidx * 4, min(gidx * 4 + 4, qi + 1)):
                            ins = t.transpose(
                                pst[:, (kb % 4) * 128:(kb % 4 + 1) * 128],
                                Pb[qi % 2][:, kb * 128:(kb + 1) * 128],
                                ident[:])
                        s_pe.tick(ins)
                        after_tgroup_g[gabs] = s_pe.n
                    P_last_reader[qi % 2] = s_pe.n

                @block.vector
                def _(v, qi=qi, h=h, base_g=base_g, ng=ng):
                    for gidx in range(ng):
                        gabs = base_g + gidx
                        v.wait_ge(s_pe.h, after_tgroup_g[gabs])
                        if qi >= 4:
                            v.wait_ge(s_pe.h, after_pv[(h, qi // 4 - 1)])
                        elif h >= 1:
                            v.wait_ge(s_pe.h, after_pv[(h - 1, 3)])
                        pst = ps_t0 if gabs % 2 == 0 else ps_t1
                        qcol = qi % 4
                        nkb = min(4, qi + 1 - gidx * 4)
                        pt3 = PT[:].rearrange("p (kb q) -> p kb q", q=512)
                        ins = v.tensor_copy(
                            pt3[:, gidx * 4:gidx * 4 + nkb, qcol * 128:(qcol + 1) * 128],
                            pst[:, 0:nkb * 128])
                        s_te.tick(ins)

            def emit_pv(qq, h=h, ostrip=ostrip):
                @block.tensor
                def _(t, qq=qq, h=h):
                    t.wait_ge(s_te.h, gcounter[0])
                    if qq >= 1:
                        t.wait_ge(s_dve.h, after_psoevac[(h, qq - 1)])
                    elif h >= 1:
                        t.wait_ge(s_dve.h, after_psoevac[(h - 1, 3)])
                    nkb = qq * 4 + 4
                    for kb in range(nkb):
                        q0 = max(0, kb * 128 - qq * 512)
                        ins = t.matmul(
                            psO[:, q0:512],
                            vtok[:, kb * 128:(kb + 1) * 128],
                            PT[:, kb * 512 + q0:(kb + 1) * 512],
                            start=(kb == 0), stop=(kb == nkb - 1))
                    s_pe.tick(ins)
                    after_pv[(h, qq)] = s_pe.n

                @block.vector
                def _(v, qq=qq, h=h, ostrip=ostrip):
                    v.wait_ge(s_pe.h, after_pv[(h, qq)])
                    if h >= 2 and qq == 0:
                        v.wait_ge(s_ds.h, oT_store[h - 2])
                    ins = v.tensor_copy(ostrip[:, qq * 512:(qq + 1) * 512], psO[:])
                    s_dve.tick(ins)
                    after_psoevac[(h, qq)] = s_dve.n

            emit_scores(0)
            for qi in range(1, NT):
                emit_softmax(qi - 1)
                emit_scores(qi)
                emit_transposes(qi - 1)
                if qi % 4 == 0:
                    emit_pv(qi // 4 - 1)
            emit_softmax(NT - 1)
            emit_transposes(NT - 1)
            emit_pv(NQ - 1)

            @block.sync
            def _(s, h=h, ostrip=ostrip):
                s.wait_ge(s_dve.h, after_psoevac[(h, 3)])
                s.dma_start(out=oT_dram[h], in_=ostrip[:]).then_inc(s_ds.h, 16)
                s_ds.n += 16
                oT_store[h] = s_ds.n

        p3_pe = s_pe.n
        p3_act = s_act.n
        p3_dve = s_dve.n
        p3_te = s_te.n

        nc.leave_named_scope('p3_attn', sc3[0], False)
        sc35 = nc.enter_named_scope('p35_gateup', False)
        for cm in reversed(p3_cms):
            cm.__exit__(None, None, None)
        p35_cms = [nc.sbuf_tensor("Wg_b0", [128, D], BF16),
                   nc.sbuf_tensor("Wg_b1", [128, D], BF16),
                   nc.sbuf_tensor("Wu_b0", [128, D], BF16),
                   nc.sbuf_tensor("Wu_b1", [128, D], BF16),
                   nc.sbuf_tensor("silu_g", [128, T], BF16),
                   nc.sbuf_tensor("h_buf0", [128, T], BF16),
                   nc.sbuf_tensor("h_buf1", [128, T], BF16)]
        Wg0, Wg1, Wu0, Wu1, silu_g, hb0, hb1 = [cm.__enter__() for cm in p35_cms]
        Wg_b = [Wg0, Wg1]
        Wu_b = [Wu0, Wu1]
        h_buf = [hb0, hb1]
        for cm in reversed(pt_cms):
            cm.__exit__(None, None, None)
        ps_up_cm = nc.psum_tensor("ps_up", [128, 2048], F32)
        ps_up = ps_up_cm.__enter__()

        gw_dma = {}
        uw_dma = {}
        after_gmm = {}
        after_umm = {}
        after_silu = {}
        after_hmul = {}
        h_store = {}

        @block.gpsimd
        def _(g):
            g.wait_ge(s_pe.h, p3_pe)
            g.wait_ge(s_ds.h, oT_store[NH - 1])
            for fc in range(2):
                g.dma_start(out=Wg_b[fc][:], in_=gw_d[fc]).then_inc(s_dp.h, 16)
                s_dp.n += 16
                gw_dma[fc] = s_dp.n
                g.dma_start(out=Wu_b[fc][:], in_=uw_d[fc]).then_inc(s_dp.h, 16)
                s_dp.n += 16
                uw_dma[fc] = s_dp.n

        for fc in range(NFC):
            @block.tensor
            def _(t, fc=fc):
                t.wait_ge(s_dp.h, gw_dma[fc])
                if fc == 0:
                    t.wait_ge(s_act.h, p3_act)
                    t.wait_ge(s_dve.h, p3_dve)
                    t.wait_ge(s_te.h, p3_te)
                else:
                    t.wait_ge(s_act.h, after_silu[fc - 1])
                for dc in range(NDC):
                    for m in range(4):
                        ins = t.matmul(
                            ps_big[:, m * 512:(m + 1) * 512],
                            Wg_b[fc % 2][:, dc * 128:(dc + 1) * 128],
                            xnT[:, dc * T + m * 512:dc * T + (m + 1) * 512],
                            start=(dc == 0), stop=(dc == NDC - 1))
                s_pe.tick(ins)
                after_gmm[fc] = s_pe.n

            @block.scalar
            def _(sc, fc=fc):
                sc.wait_ge(s_pe.h, after_gmm[fc])
                ins = sc.activation(out=silu_g[:], in_=ps_big[:],
                                    func=AF.Silu, bias=0.0,
                                    scale=gs_s[:, fc:fc + 1])
                s_act.tick(ins)
                after_silu[fc] = s_act.n

            @block.tensor
            def _(t, fc=fc):
                t.wait_ge(s_dp.h, uw_dma[fc])
                if fc >= 1:
                    t.wait_ge(s_dve.h, after_hmul[fc - 1])
                for dc in range(NDC):
                    for m in range(4):
                        ins = t.matmul(
                            ps_up[:, m * 512:(m + 1) * 512],
                            Wu_b[fc % 2][:, dc * 128:(dc + 1) * 128],
                            xnT[:, dc * T + m * 512:dc * T + (m + 1) * 512],
                            start=(dc == 0), stop=(dc == NDC - 1))
                s_pe.tick(ins)
                after_umm[fc] = s_pe.n

            @block.vector
            def _(v, fc=fc):
                v.wait_ge(s_pe.h, after_umm[fc])
                v.wait_ge(s_act.h, after_silu[fc])
                if fc >= 2:
                    v.wait_ge(s_ds.h, h_store[fc - 2])
                ins = v.scalar_tensor_tensor(
                    out=h_buf[fc % 2][:], in0=ps_up[:],
                    scalar=us_s[:, fc:fc + 1], in1=silu_g[:],
                    op0=ALU.mult, op1=ALU.mult)
                s_dve.tick(ins)
                after_hmul[fc] = s_dve.n

            @block.sync
            def _(s, fc=fc):
                s.wait_ge(s_dve.h, after_hmul[fc])
                s.dma_start(out=h_dram[fc * 128:(fc + 1) * 128, :],
                            in_=h_buf[fc % 2][:]).then_inc(s_ds.h, 16)
                s_ds.n += 16
                h_store[fc] = s_ds.n

            if fc + 2 < NFC:
                @block.gpsimd
                def _(g, fc=fc):
                    g.wait_ge(s_pe.h, after_gmm[fc])
                    g.dma_start(out=Wg_b[fc % 2][:], in_=gw_d[fc + 2]).then_inc(s_dp.h, 16)
                    s_dp.n += 16
                    gw_dma[fc + 2] = s_dp.n
                    g.wait_ge(s_pe.h, after_umm[fc])
                    g.dma_start(out=Wu_b[fc % 2][:], in_=uw_d[fc + 2]).then_inc(s_dp.h, 16)
                    s_dp.n += 16
                    uw_dma[fc + 2] = s_dp.n

        p35_pe = s_pe.n
        p35_act = s_act.n
        p35_dve = s_dve.n

        nc.leave_named_scope('p35_gateup', sc35[0], False)
        sc4 = nc.enter_named_scope('p4_oproj', False)
        for cm in reversed(p35_cms):
            cm.__exit__(None, None, None)
        xnT_cm.__exit__(None, None, None)
        p4_cms = [nc.sbuf_tensor("oT_all", [128, NH * T], BF16),
                  nc.sbuf_tensor("wo_b0", [128, D], BF16),
                  nc.sbuf_tensor("wo_b1", [128, D], BF16),
                  nc.sbuf_tensor("at_b0", [128, T], BF16),
                  nc.sbuf_tensor("at_b1", [128, T], BF16)]
        oT_all, wo_b0, wo_b1, at_b0, at_b1 = [cm.__enter__() for cm in p4_cms]
        wo_b = [wo_b0, wo_b1]
        at_b = [at_b0, at_b1]

        oT_load = {}
        wo_dma = {}
        after_omm = {}
        after_oevac = {}
        attn_store = {}

        @block.sync
        def _(s):
            s.wait_ge(s_pe.h, p35_pe)
            s.wait_ge(s_dve.h, p35_dve)
            s.wait_ge(s_ds.h, h_store[NFC - 1])
            for di in range(0, NH, 2):
                s.dma_start(out=oT_all[:, di * T:(di + 1) * T],
                            in_=oT_dram[di]).then_inc(s_ds.h, 16)
                s_ds.n += 16
                oT_load[di] = s_ds.n

        @block.scalar
        def _(sc):
            sc.wait_ge(s_pe.h, p35_pe)
            sc.wait_ge(s_dve.h, p35_dve)
            sc.wait_ge(s_ds.h, h_store[NFC - 1])
            for di in range(1, NH, 2):
                sc.dma_start(out=oT_all[:, di * T:(di + 1) * T],
                             in_=oT_dram[di]).then_inc(s_dc.h, 16)
                s_dc.n += 16
                oT_load[di] = s_dc.n

        @block.gpsimd
        def _(g):
            g.wait_ge(s_pe.h, p35_pe)
            g.wait_ge(s_ds.h, h_store[NFC - 1])
            for do in range(2):
                g.dma_start(out=wo_b[do][:], in_=wo_d[do]).then_inc(s_dp.h, 16)
                s_dp.n += 16
                wo_dma[do] = s_dp.n

        for do in range(NDC):
            psA = ps_big if do % 2 == 0 else ps_up

            @block.tensor
            def _(t, do=do, psA=psA):
                t.wait_ge(s_dp.h, wo_dma[do])
                if do == 0:
                    t.wait_ge(s_act.h, after_silu[NFC - 1])
                    t.wait_ge(s_dve.h, after_hmul[NFC - 1])
                if do >= 2:
                    t.wait_ge(s_dve.h, after_oevac[do - 2])
                if do == 0:
                    t.wait_ge(s_ds.h, oT_load[NH - 2])
                    t.wait_ge(s_dc.h, oT_load[NH - 1])
                for di in range(NH):
                    for m in range(4):
                        ins = t.matmul(
                            psA[:, m * 512:(m + 1) * 512],
                            wo_b[do % 2][:, di * 128:(di + 1) * 128],
                            oT_all[:, di * T + m * 512:di * T + (m + 1) * 512],
                            start=(di == 0), stop=(di == NH - 1))
                s_pe.tick(ins)
                after_omm[do] = s_pe.n

            @block.vector
            def _(v, do=do, psA=psA):
                v.wait_ge(s_pe.h, after_omm[do])
                if do >= 2:
                    v.wait_ge(s_ds.h, attn_store[do - 2])
                ins = v.tensor_copy(at_b[do % 2][:], psA[:])
                s_dve.tick(ins)
                after_oevac[do] = s_dve.n

            @block.sync
            def _(s, do=do):
                s.wait_ge(s_dve.h, after_oevac[do])
                s.dma_start(out=attnT_dram[do * 128:(do + 1) * 128, :],
                            in_=at_b[do % 2][:]).then_inc(s_ds.h, 16)
                s_ds.n += 16
                attn_store[do] = s_ds.n

            if do + 2 < NDC:
                @block.gpsimd
                def _(g, do=do):
                    g.wait_ge(s_pe.h, after_omm[do])
                    g.dma_start(out=wo_b[do % 2][:], in_=wo_d[do + 2]).then_inc(s_dp.h, 16)
                    s_dp.n += 16
                    wo_dma[do + 2] = s_dp.n

        p4_pe = s_pe.n
        p4_dve = s_dve.n

        nc.leave_named_scope('p4_oproj', sc4[0], False)
        sc5 = nc.enter_named_scope('p5_down', False)
        for cm in reversed(p4_cms):
            cm.__exit__(None, None, None)
        NPMAX = max(np_ for _f, np_ in PARTS)
        p5_cms = [nc.sbuf_tensor("h_part", [128, NPMAX * T], BF16),
                  nc.sbuf_tensor("Wd_b0", [128, NPMAX * 128], BF16),
                  nc.sbuf_tensor("Wd_b1", [128, NPMAX * 128], BF16),
                  nc.sbuf_tensor("dp_in0", [128, T], F32),
                  nc.sbuf_tensor("dp_in1", [128, T], F32),
                  nc.sbuf_tensor("dp_out0", [128, T], F32),
                  nc.sbuf_tensor("dp_out1", [128, T], F32),
                  nc.sbuf_tensor("dn_b0", [128, T], BF16),
                  nc.sbuf_tensor("dn_b1", [128, T], BF16)]
        (h_part, Wd_b0, Wd_b1, dp_in0, dp_in1,
         dp_out0, dp_out1, dn_b0, dn_b1) = [cm.__enter__() for cm in p5_cms]
        Wd_b = [Wd_b0, Wd_b1]
        dp_in = [dp_in0, dp_in1]
        dp_out = [dp_out0, dp_out1]
        dn_b = [dn_b0, dn_b1]

        h_load = {}
        dw_dma = {}
        after_dmm = {}
        after_devac = {}
        dpart_store = {}
        dpart_load = {}
        down_store = {}

        for p, (fc0, npart) in enumerate(PARTS):
            @block.sync
            def _(s, p=p, fc0=fc0, npart=npart):
                if p == 0:
                    s.wait_ge(s_pe.h, p4_pe)
                    s.wait_ge(s_dve.h, p4_dve)
                    s.wait_ge(s_ds.h, h_store[NFC - 1])
                else:
                    s.wait_ge(s_pe.h, after_dmm[(p - 1, NDC - 1)])
                for j in range(npart):
                    fc = fc0 + j
                    s.dma_start(out=h_part[:, j * T:(j + 1) * T],
                                in_=h_dram[fc * 128:(fc + 1) * 128, :]).then_inc(s_ds.h, 16)
                    s_ds.n += 16
                    h_load[(p, j)] = s_ds.n

            @block.gpsimd
            def _(g, p=p, fc0=fc0, npart=npart):
                for dm in range(2):
                    if p == 0:
                        g.wait_ge(s_pe.h, p4_pe)
                        g.wait_ge(s_ds.h, attn_store[NDC - 1])
                    else:
                        g.wait_ge(s_pe.h, after_dmm[(p - 1, NDC - 1)])
                    g.dma_start(out=Wd_b[dm][:, 0:npart * 128],
                                in_=dw_d[dm][:, fc0 * 128:(fc0 + npart) * 128]).then_inc(s_dp.h, 16)
                    s_dp.n += 16
                    dw_dma[(p, dm)] = s_dp.n

            for dm in range(NDC):
                psA = ps_big if dm % 2 == 0 else ps_up

                @block.tensor
                def _(t, p=p, dm=dm, psA=psA, fc0=fc0, npart=npart):
                    t.wait_ge(s_dp.h, dw_dma[(p, dm)])
                    if dm >= 2:
                        t.wait_ge(s_dve.h, after_devac[(p, dm - 2)])
                    elif p >= 1:
                        t.wait_ge(s_dve.h, after_devac[(p - 1, NDC - 2 + dm)])
                    if dm == 0:
                        t.wait_ge(s_ds.h, h_load[(p, npart - 1)])
                    for j in range(npart):
                        for m in range(4):
                            ins = t.matmul(
                                psA[:, m * 512:(m + 1) * 512],
                                Wd_b[dm % 2][:, j * 128:(j + 1) * 128],
                                h_part[:, j * T + m * 512:j * T + (m + 1) * 512],
                                start=(j == 0), stop=(j == npart - 1))
                    s_pe.tick(ins)
                    after_dmm[(p, dm)] = s_pe.n

                if p >= 1:
                    @block.scalar
                    def _(sc, p=p, dm=dm):
                        sc.wait_ge(s_ds.h, dpart_store[(p - 1, dm)])
                        if dm >= 2:
                            sc.wait_ge(s_dve.h, after_devac[(p, dm - 2)])
                        elif p >= 2:
                            sc.wait_ge(s_dve.h, after_devac[(p - 1, NDC - 2 + dm)])
                        sc.dma_start(out=dp_in[dm % 2][:],
                                     in_=dpart_dram[dm * 128:(dm + 1) * 128, :]).then_inc(s_dc.h, 16)
                        s_dc.n += 16
                        dpart_load[(p, dm)] = s_dc.n

                @block.vector
                def _(v, p=p, dm=dm, psA=psA):
                    v.wait_ge(s_pe.h, after_dmm[(p, dm)])
                    if p == 0:
                        if dm >= 2:
                            v.wait_ge(s_ds.h, dpart_store[(p, dm - 2)])
                        ins = v.tensor_copy(dp_out[dm % 2][:], psA[:])
                        s_dve.tick(ins)
                    elif p == 1:
                        v.wait_ge(s_dc.h, dpart_load[(p, dm)])
                        if dm >= 2:
                            v.wait_ge(s_ds.h, dpart_store[(p, dm - 2)])
                        else:
                            v.wait_ge(s_ds.h, dpart_store[(p - 1, NDC - 2 + dm)])
                        ins = v.tensor_tensor(out=dp_out[dm % 2][:], in0=psA[:],
                                              in1=dp_in[dm % 2][:], op=ALU.add)
                        s_dve.tick(ins)
                    else:
                        v.wait_ge(s_dc.h, dpart_load[(p, dm)])
                        if dm >= 2:
                            v.wait_ge(s_ds.h, down_store[dm - 2])
                        else:
                            v.wait_ge(s_ds.h, dpart_store[(p - 1, NDC - 2 + dm)])
                        ins = v.tensor_tensor(out=dp_out[dm % 2][:], in0=psA[:],
                                              in1=dp_in[dm % 2][:], op=ALU.add)
                        s_dve.tick(ins)
                        v.wait_ge(s_dve.h, s_dve.n)
                        ins = v.tensor_scalar(out=dn_b[dm % 2][:], in0=dp_out[dm % 2][:],
                                              scalar1=ds_s[:, dm:dm + 1], scalar2=None,
                                              op0=ALU.mult)
                        s_dve.tick(ins)
                    after_devac[(p, dm)] = s_dve.n

                @block.sync
                def _(s, p=p, dm=dm):
                    s.wait_ge(s_dve.h, after_devac[(p, dm)])
                    if p < 2:
                        s.dma_start(out=dpart_dram[dm * 128:(dm + 1) * 128, :],
                                    in_=dp_out[dm % 2][:]).then_inc(s_ds.h, 16)
                        s_ds.n += 16
                        dpart_store[(p, dm)] = s_ds.n
                    else:
                        s.dma_start(out=downT_dram[dm * 128:(dm + 1) * 128, :],
                                    in_=dn_b[dm % 2][:]).then_inc(s_ds.h, 16)
                        s_ds.n += 16
                        down_store[dm] = s_ds.n

                if dm + 2 < NDC:
                    @block.gpsimd
                    def _(g, p=p, dm=dm, fc0=fc0, npart=npart):
                        g.wait_ge(s_pe.h, after_dmm[(p, dm)])
                        g.dma_start(out=Wd_b[dm % 2][:, 0:npart * 128],
                                    in_=dw_d[dm + 2][:, fc0 * 128:(fc0 + npart) * 128]).then_inc(s_dp.h, 16)
                        s_dp.n += 16
                        dw_dma[(p, dm + 2)] = s_dp.n

        p5_pe = s_pe.n
        p5_dve = s_dve.n

        nc.leave_named_scope('p5_down', sc5[0], False)
        sc6 = nc.enter_named_scope('p6_final', False)
        for cm in reversed(p5_cms):
            cm.__exit__(None, None, None)
        p6_cms = [nc.sbuf_tensor("x6_0", [128, D], F32),
                  nc.sbuf_tensor("x6_1", [128, D], F32),
                  nc.sbuf_tensor("a6_0", [128, D], BF16),
                  nc.sbuf_tensor("a6_1", [128, D], BF16),
                  nc.sbuf_tensor("d6_0", [128, D], BF16),
                  nc.sbuf_tensor("d6_1", [128, D], BF16),
                  nc.sbuf_tensor("o6_0", [128, D], F32),
                  nc.sbuf_tensor("o6_1", [128, D], F32)]
        x6_0, x6_1, a6_0, a6_1, d6_0, d6_1, o6_0, o6_1 = [cm.__enter__() for cm in p6_cms]
        x6 = [x6_0, x6_1]
        a6 = [a6_0, a6_1]
        d6 = [d6_0, d6_1]
        o6 = [o6_0, o6_1]

        p6_x = {}
        p6_a = {}
        p6_d = {}
        p6_sum = {}
        out_store = {}

        for i in range(NT):
            @block.scalar
            def _(sc, i=i):
                if i < 2:
                    sc.wait_ge(s_pe.h, p5_pe)
                    sc.wait_ge(s_dve.h, p5_dve)
                    sc.wait_ge(s_ds.h, down_store[NDC - 1])
                if i >= 2:
                    sc.wait_ge(s_dve.h, p6_sum[i - 2])
                sc.dma_start(out=x6[i % 2][:],
                             in_=x_d[i * 128:(i + 1) * 128, :]).then_inc(s_dc.h, 16)
                s_dc.n += 16
                p6_x[i] = s_dc.n

            @block.sync
            def _(s, i=i):
                if i < 2:
                    s.wait_ge(s_pe.h, p5_pe)
                    s.wait_ge(s_dve.h, p5_dve)
                    s.wait_ge(s_ds.h, attn_store[NDC - 1])
                    s.wait_ge(s_ds.h, down_store[NDC - 1])
                if i >= 2:
                    s.wait_ge(s_dve.h, p6_sum[i - 2])
                s.dma_start_transpose(a6[i % 2][:],
                                      attnT_dram[:, i * 128:(i + 1) * 128]).then_inc(s_ds.h, 16)
                s_ds.n += 16
                p6_a[i] = s_ds.n
                s.dma_start_transpose(d6[i % 2][:],
                                      downT_dram[:, i * 128:(i + 1) * 128]).then_inc(s_ds.h, 16)
                s_ds.n += 16
                p6_d[i] = s_ds.n

            @block.vector
            def _(v, i=i):
                v.wait_ge(s_ds.h, p6_d[i])
                v.wait_ge(s_dc.h, p6_x[i])
                if i >= 2:
                    v.wait_ge(s_ds.h, out_store[i - 2])
                ins = v.tensor_tensor(out=o6[i % 2][:], in0=a6[i % 2][:],
                                      in1=d6[i % 2][:], op=ALU.add)
                s_dve.tick(ins)
                v.wait_ge(s_dve.h, s_dve.n)
                ins = v.tensor_tensor(out=o6[i % 2][:], in0=o6[i % 2][:],
                                      in1=x6[i % 2][:], op=ALU.add)
                s_dve.tick(ins)
                p6_sum[i] = s_dve.n

            @block.sync
            def _(s, i=i):
                s.wait_ge(s_dve.h, p6_sum[i])
                s.dma_start(out=out_d[i * 128:(i + 1) * 128, :],
                            in_=o6[i % 2][:]).then_inc(s_ds.h, 16)
                s_ds.n += 16
                out_store[i] = s_ds.n

        for cm in reversed(p6_cms):
            cm.__exit__(None, None, None)
        ps_up_cm.__exit__(None, None, None)

        nc.leave_named_scope('p6_final', sc6[0], False)
        @block.sync
        def _(s):
            s.wait_ge(s_ds.h, out_store[NT - 1])

    return nc


def _host_layout(wq, wk, wv, wo, gw, uw, dw, gs, us, ds):
    wq_p = np.ascontiguousarray(
        wq.reshape(NDC, 128, NH, 128).transpose(2, 1, 0, 3).reshape(NH, 128, D))
    wo_p = np.ascontiguousarray(
        wo.reshape(NDC, 128, NDC, 128).transpose(2, 1, 0, 3).reshape(NDC, 128, D))
    wk_p = np.ascontiguousarray(
        wk.reshape(NDC, 128, 128).transpose(1, 0, 2).reshape(128, D))
    wv_p = np.ascontiguousarray(
        wv.reshape(NDC, 128, 128).transpose(1, 0, 2).reshape(128, D))
    gw_p = np.ascontiguousarray(
        gw.reshape(NFC, 128, NDC, 128).transpose(0, 3, 2, 1).reshape(NFC, 128, D))
    uw_p = np.ascontiguousarray(
        uw.reshape(NFC, 128, NDC, 128).transpose(0, 3, 2, 1).reshape(NFC, 128, D))
    dw_p = np.ascontiguousarray(
        dw.reshape(NDC, 128, NFC, 128).transpose(0, 3, 2, 1).reshape(NDC, 128, DFF))
    gs_p = np.ascontiguousarray(gs.reshape(NFC, 128).T)
    us_p = np.ascontiguousarray(us.reshape(NFC, 128).T)
    ds_p = np.ascontiguousarray(ds.reshape(NDC, 128).T)
    return wq_p, wk_p, wv_p, wo_p, gw_p, uw_p, dw_p, gs_p, us_p, ds_p


def kernel(L0_attn_x, L0_attn_wq, L0_attn_wk, L0_attn_wv, L0_attn_wo,
           L0_ffn_gate_w, L0_ffn_gate_s, L0_ffn_up_w, L0_ffn_up_s,
           L0_ffn_down_w, L0_ffn_down_s):
    x = np.ascontiguousarray(np.asarray(L0_attn_x, dtype=np.float32))
    wq = np.asarray(L0_attn_wq, dtype=np.float32)
    wk = np.asarray(L0_attn_wk, dtype=np.float32)
    wv = np.asarray(L0_attn_wv, dtype=np.float32)
    wo = np.asarray(L0_attn_wo, dtype=np.float32)
    gw = np.asarray(L0_ffn_gate_w, dtype=np.int32)
    uw = np.asarray(L0_ffn_up_w, dtype=np.int32)
    dw = np.asarray(L0_ffn_down_w, dtype=np.int32)
    gs = np.asarray(L0_ffn_gate_s, dtype=np.float32)
    us = np.asarray(L0_ffn_up_s, dtype=np.float32)
    ds = np.asarray(L0_ffn_down_s, dtype=np.float32)

    (wq_p, wk_p, wv_p, wo_p, gw_p, uw_p, dw_p,
     gs_p, us_p, ds_p) = _host_layout(wq, wk, wv, wo, gw, uw, dw, gs, us, ds)

    nc = build_nc()
    in_maps = []
    for c in range(8):
        in_maps.append(dict(
            x=np.ascontiguousarray(x[c]),
            wq_p=wq_p, wk_p=wk_p, wv_p=wv_p, wo_p=wo_p,
            gw_p=gw_p, uw_p=uw_p, dw_p=dw_p,
            gs_p=gs_p, us_p=us_p, ds_p=ds_p,
        ))
    res = run_bass_kernel_spmd(nc, in_maps, core_ids=list(range(8)))
    out = np.stack([res.results[c]["out"] for c in range(8)], axis=0)
    return out


# version 04
# speedup vs baseline: 1.1598x; 1.1598x over previous
import numpy as np
from contextlib import ExitStack

import concourse.bass as bass
import concourse.mybir as mybir
from concourse.bass_utils import run_bass_kernel_spmd

F32 = mybir.dt.float32
BF16 = mybir.dt.bfloat16
I32 = mybir.dt.int32
AF = mybir.ActivationFunctionType
ALU = mybir.AluOpType
AX = mybir.AxisListType

T = 2048
D = 4096
DFF = 11008
NH = 32
NT = T // 128
NDC = D // 128
NFC = DFF // 128
NQ = T // 512
C_SCALE = float(1.0 / np.sqrt(128.0))
EXP_BIAS = -12.0
LN_EPS = 1e-5
INV_D = 1.0 / D
PARTS = [(0, 29), (29, 29), (58, 28)]


def seg_list(qi):
    kend = (qi + 1) * 128
    if kend <= 1024:
        return [(0, kend)]
    return [(0, 1024), (1024, kend - 1024)]


def n_tgroups(qi):
    return -(-(qi + 1) // 4)


SEGS = []
SI_OF = {}
LAST_SI = {}
for _qi in range(NT):
    _sl = seg_list(_qi)
    for _j, (_s, _l) in enumerate(_sl):
        SI_OF[(_qi, _j)] = len(SEGS)
        SEGS.append((_qi, _j, _s, _l))
        LAST_SI[_qi] = len(SEGS) - 1
NSEG = len(SEGS)
TGC = [0]
for _qi in range(NT):
    TGC.append(TGC[-1] + n_tgroups(_qi))
NTG = TGC[-1]
NEV = NTG // 2
CMB_CUM = [0]
for _qi in range(NT):
    CMB_CUM.append(CMB_CUM[-1] + (1 if _qi < 8 else 2))


def dev_F(p, dm):
    if p == 0:
        return dm + 1
    if p == 1:
        return 32 + dm + 1
    return 64 + 2 * (dm + 1)


class Sem:
    def __init__(self, nc, e, name):
        self.h = e(nc.semaphore(name))


def w16(eng, sems, k):
    eng.wait_ge(sems[k % 2].h, 16 * (k // 2 + 1))


def wall16(eng, sems, n):
    eng.wait_ge(sems[0].h, 16 * ((n + 1) // 2))
    if n // 2:
        eng.wait_ge(sems[1].h, 16 * (n // 2))


def i16(ins, sems, k):
    ins.then_inc(sems[k % 2].h, 16)


def build_nc():
    nc = bass.Bass("TRN2", target_bir_lowering=False)

    x_d = nc.dram_tensor("x", [T, D], F32, kind="ExternalInput").ap()
    wq_d = nc.dram_tensor("wq_p", [NH, 128, D], F32, kind="ExternalInput").ap()
    wk_d = nc.dram_tensor("wk_p", [128, D], F32, kind="ExternalInput").ap()
    wv_d = nc.dram_tensor("wv_p", [128, D], F32, kind="ExternalInput").ap()
    wo_d = nc.dram_tensor("wo_p", [NDC, 128, D], F32, kind="ExternalInput").ap()
    gw_d = nc.dram_tensor("gw_p", [NFC, 128, D], I32, kind="ExternalInput").ap()
    uw_d = nc.dram_tensor("uw_p", [NFC, 128, D], I32, kind="ExternalInput").ap()
    dw_d = nc.dram_tensor("dw_p", [NDC, 128, DFF], I32, kind="ExternalInput").ap()
    gs_d = nc.dram_tensor("gs_p", [128, NFC], F32, kind="ExternalInput").ap()
    us_d = nc.dram_tensor("us_p", [128, NFC], F32, kind="ExternalInput").ap()
    ds_d = nc.dram_tensor("ds_p", [128, NDC], F32, kind="ExternalInput").ap()
    out_d = nc.dram_tensor("out", [T, D], F32, kind="ExternalOutput").ap()

    xn_dram = nc.dram_tensor("xn_dram", [T, D], BF16, kind="Internal").ap()
    oT_dram = nc.dram_tensor("oT_dram", [NH, 128, T], BF16, kind="Internal").ap()
    h_dram = nc.dram_tensor("h_dram", [DFF, T], BF16, kind="Internal").ap()
    attnT_dram = nc.dram_tensor("attnT_dram", [D, T], BF16, kind="Internal").ap()
    downT_dram = nc.dram_tensor("downT_dram", [D, T], BF16, kind="Internal").ap()
    dpart_dram = nc.dram_tensor("dpart_dram", [D, T], F32, kind="Internal").ap()

    with ExitStack() as ctx:
        e = ctx.enter_context

        cst = e(nc.sbuf_tensor("cst", [128, 8], F32))
        maskA = e(nc.sbuf_tensor("maskA", [128, 128], F32))
        ident = e(nc.sbuf_tensor("ident", [128, 128], BF16))
        gs_s = e(nc.sbuf_tensor("gs_s", [128, NFC], F32))
        us_s = e(nc.sbuf_tensor("us_s", [128, NFC], F32))
        ds_s = e(nc.sbuf_tensor("ds_s", [128, NDC], F32))
        kT = e(nc.sbuf_tensor("kT", [128, T], BF16))
        vtok = e(nc.sbuf_tensor("vtok", [128, T], BF16))
        sums0 = e(nc.sbuf_tensor("sums0", [128, 64], F32))
        sums1 = e(nc.sbuf_tensor("sums1", [128, 64], F32))
        stat = e(nc.sbuf_tensor("stat", [128, 16], F32))

        S = lambda name: Sem(nc, e, name)
        s_dsc = S("s_dsc")
        s_dxl = [S("s_dxl0"), S("s_dxl1")]
        s_dxn = [S("s_dxn0"), S("s_dxn1")]
        s_dxt = S("s_dxt")
        s_dwkv = S("s_dwkv")
        s_dwq = [S("s_dwq0"), S("s_dwq1")]
        s_dgw = [S("s_dgw0"), S("s_dgw1")]
        s_duw = [S("s_duw0"), S("s_duw1")]
        s_dwo = [S("s_dwo0"), S("s_dwo1")]
        s_ddw = [S("s_ddw0"), S("s_ddw1")]
        s_doS = [S("s_doS0"), S("s_doS1")]
        s_dolS = S("s_dolS")
        s_dolC = S("s_dolC")
        s_dhS = [S("s_dhS0"), S("s_dhS1")]
        s_dhL = S("s_dhL")
        s_daS = [S("s_daS0"), S("s_daS1")]
        s_dpS = [S("s_dpS0"), S("s_dpS1")]
        s_dpL = [S("s_dpL0"), S("s_dpL1")]
        s_dnS = [S("s_dnS0"), S("s_dnS1")]
        s_d6 = [S("s_d60"), S("s_d61")]
        s_dx6 = [S("s_dx60"), S("s_dx61")]
        s_dout = [S("s_dout0"), S("s_dout1")]
        s_pool = S("s_pool")
        s_lnV = S("s_lnV")
        s_lnA = S("s_lnA")
        s_kv = S("s_kv")
        s_kve = S("s_kve")
        s_vtr = S("s_vtr")
        s_vte = S("s_vte")
        s_q = S("s_q")
        s_qe = S("s_qe")
        s_sc = S("s_sc")
        s_exp = S("s_exp")
        s_msk = S("s_msk")
        s_cmb = S("s_cmb")
        s_psc = S("s_psc")
        s_tg = S("s_tg")
        s_tevD = S("s_tevD")
        s_tevA = S("s_tevA")
        s_pv = S("s_pv")
        s_pso = S("s_pso")
        s_gmm = S("s_gmm")
        s_umm = S("s_umm")
        s_silu = S("s_silu")
        s_hm = S("s_hm")
        s_omm = S("s_omm")
        s_oev = S("s_oev")
        s_dmm = S("s_dmm")
        s_dev = S("s_dev")
        s_sum = S("s_sum")

        block = e(nc.Block())

        consts = [EXP_BIAS, C_SCALE, 0.0, 1.0, LN_EPS, -0.5]
        for i, cv in enumerate(consts):
            nc.const_aps.register(F32, cv, cst[:, i:i + 1])

        p1_cms = [nc.sbuf_tensor("xt0", [128, D], F32, side="right"),
                  nc.sbuf_tensor("xt1", [128, D], F32, side="right"),
                  nc.sbuf_tensor("xnb0", [128, D], BF16, side="right"),
                  nc.sbuf_tensor("xnb1", [128, D], BF16, side="right"),
                  nc.sbuf_tensor("sqscr", [128, D], BF16, side="right")]
        xt0, xt1, xnb0, xnb1, sqscr = [cm.__enter__() for cm in p1_cms]
        xt = [xt0, xt1]
        xnb = [xnb0, xnb1]

        @block.gpsimd
        def _(g):
            for i, cv in enumerate(consts):
                g.memset(cst[:, i:i + 1], cv)
            g.memset(maskA[:], 0.0).then_inc(s_pool.h, 1)
            g.memset(ident[:], 1.0).then_inc(s_pool.h, 1)
            g.wait_ge(s_pool.h, 2)
            g.affine_select(out=maskA[:], in_=maskA[:], pattern=[[-1, 128]],
                            compare_op=ALU.is_ge, fill=-1e30, base=0,
                            channel_multiplier=1).then_inc(s_pool.h, 1)
            g.wait_ge(s_pool.h, 3)
            g.affine_select(out=ident[:], in_=ident[:], pattern=[[1, 128]],
                            compare_op=ALU.is_equal, fill=0.0, base=0,
                            channel_multiplier=-1).then_inc(s_pool.h, 1)
            for src, dst in ((gs_d, gs_s), (us_d, us_s), (ds_d, ds_s)):
                g.dma_start(out=dst[:], in_=src).then_inc(s_dsc.h, 16)

        sc1 = nc.enter_named_scope('p1_ln', False)

        @block.sync
        def _(s):
            def xn_store(j):
                s.wait_ge(s_lnV.h, 5 * j + 5)
                s.dma_start(out=xn_dram[j * 128:(j + 1) * 128, :],
                            in_=xnb[j % 2][:]).then_inc(s_dxn[j % 2].h, 16)
            for i in range(NT):
                if i >= 2:
                    xn_store(i - 2)
                s.dma_start(out=xt[i % 2][:],
                            in_=x_d[i * 128:(i + 1) * 128, :]).then_inc(s_dxl[i % 2].h, 16)
            xn_store(NT - 2)
            xn_store(NT - 1)

        @block.vector
        def _(v):
            v.wait_ge(s_pool.h, 4)
            for i in range(NT):
                c0 = (i % 2) * 8
                w16(v, s_dxl, i)
                v.reduce_sum(stat[:, c0:c0 + 1], xt[i % 2][:], axis=AX.X).then_inc(s_lnV.h, 1)
                v.wait_ge(s_lnV.h, 5 * i + 1)
                v.tensor_scalar(out=stat[:, c0 + 1:c0 + 2], in0=stat[:, c0:c0 + 1],
                                scalar1=-INV_D, scalar2=None, op0=ALU.mult).then_inc(s_lnV.h, 1)
                v.wait_ge(s_lnV.h, 5 * i + 2)
                v.tensor_tensor(out=stat[:, c0 + 2:c0 + 3], in0=stat[:, c0 + 1:c0 + 2],
                                in1=stat[:, c0 + 1:c0 + 2], op=ALU.mult).then_inc(s_lnV.h, 1)
                v.wait_ge(s_lnA.h, 3 * i + 1)
                v.wait_ge(s_lnV.h, 5 * i + 3)
                v.tensor_scalar(out=stat[:, c0 + 4:c0 + 5], in0=stat[:, c0 + 3:c0 + 4],
                                scalar1=INV_D, scalar2=stat[:, c0 + 2:c0 + 3],
                                op0=ALU.mult, op1=ALU.subtract).then_inc(s_lnV.h, 1)
                v.wait_ge(s_lnA.h, 3 * i + 3)
                if i >= 2:
                    w16(v, s_dxn, i - 2)
                v.tensor_scalar(out=xnb[i % 2][:], in0=xt[i % 2][:],
                                scalar1=stat[:, c0 + 1:c0 + 2],
                                scalar2=stat[:, c0 + 6:c0 + 7],
                                op0=ALU.add, op1=ALU.mult).then_inc(s_lnV.h, 1)

        @block.scalar
        def _(sc):
            sc.wait_ge(s_pool.h, 4)
            for i in range(NT):
                c0 = (i % 2) * 8
                w16(sc, s_dxl, i)
                if i >= 2:
                    sc.wait_ge(s_lnV.h, 5 * (i - 2) + 4)
                sc.activation(out=sqscr[:], in_=xt[i % 2][:], func=AF.Square,
                              bias=0.0, scale=1.0,
                              accum_out=stat[:, c0 + 3:c0 + 4]).then_inc(s_lnA.h, 1)
                sc.wait_ge(s_lnV.h, 5 * i + 4)
                sc.activation(out=stat[:, c0 + 5:c0 + 6], in_=stat[:, c0 + 4:c0 + 5],
                              func=AF.Ln, bias=LN_EPS, scale=1.0).then_inc(s_lnA.h, 1)
                sc.wait_ge(s_lnA.h, 3 * i + 2)
                sc.activation(out=stat[:, c0 + 6:c0 + 7], in_=stat[:, c0 + 5:c0 + 6],
                              func=AF.Exp, bias=0.0, scale=-0.5).then_inc(s_lnA.h, 1)

        nc.leave_named_scope('p1_ln', sc1[0], False)

        for cm in reversed(p1_cms):
            cm.__exit__(None, None, None)
        xnT_cm = nc.sbuf_tensor("xnT", [128, NDC * T], BF16)
        xnT = xnT_cm.__enter__()
        p3_cms = [nc.sbuf_tensor("wq_b0", [128, D], BF16, side="right"),
                  nc.sbuf_tensor("wq_b1", [128, D], BF16, side="right"),
                  nc.sbuf_tensor("qT_h", [128, T], BF16, side="right"),
                  nc.sbuf_tensor("Pb0", [128, T], BF16, side="right"),
                  nc.sbuf_tensor("Pb1", [128, T], BF16, side="right"),
                  nc.sbuf_tensor("Pb2", [128, T], BF16, side="right"),
                  nc.sbuf_tensor("Pb3", [128, T], BF16, side="right"),
                  nc.sbuf_tensor("PT", [128, NT * 512], BF16, side="right"),
                  nc.sbuf_tensor("oT_strip0", [128, T], BF16, side="right"),
                  nc.sbuf_tensor("oT_strip1", [128, T], BF16, side="right")]
        (wq_b0, wq_b1, qT_h, Pb0, Pb1, Pb2, Pb3, PT,
         oT_s0, oT_s1) = [cm.__enter__() for cm in p3_cms]
        wq_b = [wq_b0, wq_b1]
        Pb = [Pb0, Pb1, Pb2, Pb3]
        oT_strip = [oT_s0, oT_s1]
        vT_tmp = qT_h
        wkv_b = PT
        ps_big = e(nc.psum_tensor("ps_big", [128, 2048], F32))
        pt_cms = [nc.psum_tensor("ps_t0", [128, 1024], BF16),
                  nc.psum_tensor("ps_t1", [128, 1024], BF16),
                  nc.psum_tensor("psO", [128, 512], F32)]
        ps_t0, ps_t1, psO = [cm.__enter__() for cm in pt_cms]
        pt3 = PT[:].rearrange("p (kb q) -> p kb q", q=512)

        @block.sync
        def _(s):
            wall16(s, s_dxn, NT)
            for dc in range(NDC):
                s.dma_start_transpose(xnT[:, dc * T:(dc + 1) * T],
                                      xn_dram[:, dc * 128:(dc + 1) * 128]).then_inc(s_dxt.h, 16)

        @block.gpsimd
        def _(g):
            wall16(g, s_dxl, NT)
            wall16(g, s_dxn, NT)
            g.wait_ge(s_lnV.h, 5 * NT)
            g.wait_ge(s_lnA.h, 3 * NT)
            g.dma_start(out=wkv_b[:, 0:D], in_=wk_d).then_inc(s_dwkv.h, 16)
            g.dma_start(out=wkv_b[:, D:2 * D], in_=wv_d).then_inc(s_dwkv.h, 16)
            for h in range(NH):
                if h >= 2:
                    g.wait_ge(s_q.h, h - 1)
                g.dma_start(out=wq_b[h % 2][:], in_=wq_d[h]).then_inc(s_dwq[h % 2].h, 16)

        sc2 = nc.enter_named_scope('p2_kv', False)

        @block.tensor
        def _(t):
            t.wait_ge(s_pool.h, 4)
            t.wait_ge(s_dwkv.h, 32)
            t.wait_ge(s_dxt.h, 16 * NDC)
            for which in range(2):
                if which == 1:
                    t.wait_ge(s_kve.h, 1)
                for dc in range(NDC):
                    for m in range(4):
                        ins = t.matmul(
                            ps_big[:, m * 512:(m + 1) * 512],
                            wkv_b[:, which * D + dc * 128:which * D + (dc + 1) * 128],
                            xnT[:, dc * T + m * 512:dc * T + (m + 1) * 512],
                            start=(dc == 0), stop=(dc == NDC - 1))
                ins.then_inc(s_kv.h, 1)
            for kb in range(NT):
                if kb == 0:
                    t.wait_ge(s_kve.h, 2)
                if kb >= 2:
                    t.wait_ge(s_vte.h, kb - 1)
                pst = ps_t0 if kb % 2 == 0 else ps_t1
                t.transpose(pst[:, 0:128], vT_tmp[:, kb * 128:(kb + 1) * 128],
                            ident[:]).then_inc(s_vtr.h, 1)

        @block.vector
        def _(v):
            for which in range(2):
                v.wait_ge(s_kv.h, which + 1)
                dst = kT if which == 0 else vT_tmp
                v.tensor_copy(dst[:], ps_big[:]).then_inc(s_kve.h, 1)
            for kb in range(NT):
                v.wait_ge(s_vtr.h, kb + 1)
                pst = ps_t0 if kb % 2 == 0 else ps_t1
                v.tensor_copy(vtok[:, kb * 128:(kb + 1) * 128],
                              pst[:, 0:128]).then_inc(s_vte.h, 1)

        nc.leave_named_scope('p2_kv', sc2[0], False)
        sc3 = nc.enter_named_scope('p3_attn', False)

        @block.tensor
        def _(t):
            t.wait_ge(s_pool.h, 4)
            for h in range(NH):
                w16(t, s_dwq, h)
                if h == 0:
                    t.wait_ge(s_vte.h, NT)
                else:
                    t.wait_ge(s_exp.h, h * NSEG)
                for dc in range(NDC):
                    for m in range(4):
                        ins = t.matmul(
                            ps_big[:, m * 512:(m + 1) * 512],
                            wq_b[h % 2][:, dc * 128:(dc + 1) * 128],
                            xnT[:, dc * T + m * 512:dc * T + (m + 1) * 512],
                            start=(dc == 0), stop=(dc == NDC - 1))
                ins.then_inc(s_q.h, 1)

                def emit_scores(qi, h=h):
                    for j, (st0, ln) in enumerate(seg_list(qi)):
                        si = SI_OF[(qi, j)]
                        half = si % 2
                        if si == 0:
                            t.wait_ge(s_qe.h, h + 1)
                        if si >= 2:
                            t.wait_ge(s_exp.h, h * NSEG + si - 1)
                        for m in range(-(-ln // 512)):
                            o0 = m * 512
                            w = min(512, ln - o0)
                            ins = t.matmul(
                                ps_big[:, half * 1024 + o0:half * 1024 + o0 + w],
                                qT_h[:, qi * 128:(qi + 1) * 128],
                                kT[:, st0 + o0:st0 + o0 + w],
                                start=True, stop=True)
                        ins.then_inc(s_sc.h, 1)

                def emit_transposes(qi, h=h):
                    t.wait_ge(s_psc.h, h * NT + qi + 1)
                    for gidx in range(n_tgroups(qi)):
                        g = h * NTG + TGC[qi] + gidx
                        if g >= 2:
                            if g % 2 == 0:
                                t.wait_ge(s_tevD.h, g // 2)
                            else:
                                t.wait_ge(s_tevA.h, g // 2)
                        pst = ps_t0 if g % 2 == 0 else ps_t1
                        for kb in range(gidx * 4, min(gidx * 4 + 4, qi + 1)):
                            ins = t.transpose(
                                pst[:, (kb % 4) * 128:(kb % 4 + 1) * 128],
                                Pb[qi % 4][:, kb * 128:(kb + 1) * 128],
                                ident[:])
                        ins.then_inc(s_tg.h, 1)

                def emit_pv(qq, h=h):
                    G = h * NTG + TGC[qq * 4 + 4]
                    t.wait_ge(s_tevD.h, (G + 1) // 2)
                    t.wait_ge(s_tevA.h, G // 2)
                    t.wait_ge(s_pso.h, h * NQ + qq)
                    nkb = qq * 4 + 4
                    for kb in range(nkb):
                        q0 = max(0, kb * 128 - qq * 512)
                        ins = t.matmul(
                            psO[:, q0:512],
                            vtok[:, kb * 128:(kb + 1) * 128],
                            PT[:, kb * 512 + q0:(kb + 1) * 512],
                            start=(kb == 0), stop=(kb == nkb - 1))
                    ins.then_inc(s_pv.h, 1)

                emit_scores(0)
                for qi in range(1, NT):
                    emit_scores(qi)
                    emit_transposes(qi - 1)
                    if qi % 4 == 0:
                        emit_pv(qi // 4 - 1)
                emit_transposes(NT - 1)
                emit_pv(NQ - 1)

        @block.vector
        def _(v):
            v.wait_ge(s_pool.h, 4)
            for h in range(NH):
                sums = sums0 if h % 2 == 0 else sums1
                cmb_base = h * CMB_CUM[NT]

                def evacsD(qe, h=h):
                    for gidx in range(n_tgroups(qe)):
                        g = h * NTG + TGC[qe] + gidx
                        if g % 2 != 0:
                            continue
                        v.wait_ge(s_tg.h, g + 1)
                        v.wait_ge(s_pv.h, h * NQ + qe // 4)
                        qcol = qe % 4
                        nkb = min(4, qe + 1 - gidx * 4)
                        v.tensor_copy(
                            pt3[:, gidx * 4:gidx * 4 + nkb,
                                qcol * 128:(qcol + 1) * 128],
                            ps_t0[:, 0:nkb * 128]).then_inc(s_tevD.h, 1)

                for qi in range(NT):
                    si = LAST_SI[qi]
                    _q, j, st0, ln = SEGS[si]
                    half = si % 2
                    dcol = qi * 128 - st0
                    v.wait_ge(s_sc.h, h * NSEG + si + 1)
                    if h >= 2 and qi == 0:
                        v.wait_ge(s_exp.h, (h - 1) * NSEG)
                    v.tensor_tensor(
                        out=ps_big[:, half * 1024 + dcol:half * 1024 + dcol + 128],
                        in0=ps_big[:, half * 1024 + dcol:half * 1024 + dcol + 128],
                        in1=maskA[:], op=ALU.add).then_inc(s_msk.h, 1)
                    v.wait_ge(s_exp.h, h * NSEG + si + 1)
                    if qi >= 8:
                        v.tensor_tensor(out=sums[:, 32 + qi:32 + qi + 1],
                                        in0=sums[:, 0 + qi:0 + qi + 1],
                                        in1=sums[:, 16 + qi:16 + qi + 1],
                                        op=ALU.add).then_inc(s_cmb.h, 1)
                        v.wait_ge(s_cmb.h, cmb_base + CMB_CUM[qi] + 1)
                        srcc = sums[:, 32 + qi:32 + qi + 1]
                    else:
                        srcc = sums[:, 0 + qi:0 + qi + 1]
                    v.reciprocal(sums[:, 48 + qi:48 + qi + 1], srcc).then_inc(s_cmb.h, 1)
                    v.wait_ge(s_cmb.h, cmb_base + CMB_CUM[qi + 1])
                    kend = (qi + 1) * 128
                    v.tensor_scalar(
                        out=Pb[qi % 4][:, 0:kend], in0=Pb[qi % 4][:, 0:kend],
                        scalar1=sums[:, 48 + qi:48 + qi + 1], scalar2=None,
                        op0=ALU.mult).then_inc(s_psc.h, 1)
                    if qi >= 1:
                        evacsD(qi - 1)
                evacsD(NT - 1)

        @block.scalar
        def _(sc):
            for h in range(NH):
                sums = sums0 if h % 2 == 0 else sums1
                sc.wait_ge(s_q.h, h + 1)
                sc.activation(out=qT_h[:], in_=ps_big[:], func=AF.Copy,
                              bias=0.0, scale=1.0).then_inc(s_qe.h, 1)

                def emit_exp(qi, h=h, sums=sums):
                    for j, (st0, ln) in enumerate(seg_list(qi)):
                        si = SI_OF[(qi, j)]
                        half = si % 2
                        if si == LAST_SI[qi]:
                            sc.wait_ge(s_msk.h, h * NT + qi + 1)
                        else:
                            sc.wait_ge(s_sc.h, h * NSEG + si + 1)
                        if qi >= 4:
                            sc.wait_ge(s_tg.h, h * NTG + TGC[qi - 3])
                        elif h >= 1:
                            sc.wait_ge(s_tg.h, (h - 1) * NTG + TGC[qi + 13])
                        if h >= 2 and qi == 0 and j == 0:
                            sc.wait_ge(s_cmb.h, (h - 1) * CMB_CUM[NT])
                        sc.activation(
                            out=Pb[qi % 4][:, st0:st0 + ln],
                            in_=ps_big[:, half * 1024:half * 1024 + ln],
                            func=AF.Exp, bias=EXP_BIAS, scale=C_SCALE,
                            accum_out=sums[:, 16 * j + qi:16 * j + qi + 1]).then_inc(s_exp.h, 1)

                def evacsA(qe, h=h):
                    for gidx in range(n_tgroups(qe)):
                        g = h * NTG + TGC[qe] + gidx
                        if g % 2 != 1:
                            continue
                        sc.wait_ge(s_tg.h, g + 1)
                        sc.wait_ge(s_pv.h, h * NQ + qe // 4)
                        qcol = qe % 4
                        nkb = min(4, qe + 1 - gidx * 4)
                        sc.activation(
                            out=pt3[:, gidx * 4:gidx * 4 + nkb,
                                    qcol * 128:(qcol + 1) * 128],
                            in_=ps_t1[:, 0:nkb * 128],
                            func=AF.Copy, bias=0.0, scale=1.0).then_inc(s_tevA.h, 1)

                def psoevac(qq, h=h):
                    sc.wait_ge(s_pv.h, h * NQ + qq + 1)
                    if h >= 2 and qq == 0:
                        w16(sc, s_doS, h - 2)
                    sc.activation(out=oT_strip[h % 2][:, qq * 512:(qq + 1) * 512],
                                  in_=psO[:], func=AF.Copy, bias=0.0,
                                  scale=1.0).then_inc(s_pso.h, 1)

                emit_exp(0)
                for qi in range(1, NT):
                    emit_exp(qi)
                    evacsA(qi - 1)
                    if qi % 4 == 0:
                        psoevac(qi // 4 - 1)
                evacsA(NT - 1)
                psoevac(NQ - 1)

        @block.sync
        def _(s):
            for h in range(NH):
                s.wait_ge(s_pso.h, h * NQ + NQ)
                s.dma_start(out=oT_dram[h], in_=oT_strip[h % 2][:]).then_inc(s_doS[h % 2].h, 16)

        nc.leave_named_scope('p3_attn', sc3[0], False)

        for cm in reversed(p3_cms):
            cm.__exit__(None, None, None)
        p35_cms = [nc.sbuf_tensor("Wg_b0", [128, D], BF16, side="right"),
                   nc.sbuf_tensor("Wg_b1", [128, D], BF16, side="right"),
                   nc.sbuf_tensor("Wu_b0", [128, D], BF16, side="right"),
                   nc.sbuf_tensor("Wu_b1", [128, D], BF16, side="right"),
                   nc.sbuf_tensor("silu_g", [128, T], BF16, side="right"),
                   nc.sbuf_tensor("h_buf0", [128, T], BF16, side="right"),
                   nc.sbuf_tensor("h_buf1", [128, T], BF16, side="right")]
        Wg0, Wg1, Wu0, Wu1, silu_g, hb0, hb1 = [cm.__enter__() for cm in p35_cms]
        Wg_b = [Wg0, Wg1]
        Wu_b = [Wu0, Wu1]
        h_buf = [hb0, hb1]
        for cm in reversed(pt_cms):
            cm.__exit__(None, None, None)
        ps_up_cm = nc.psum_tensor("ps_up", [128, 2048], F32)
        ps_up = ps_up_cm.__enter__()

        @block.gpsimd
        def _(g):
            g.wait_ge(s_exp.h, NH * NSEG)
            g.wait_ge(s_tevD.h, NH * NEV)
            g.wait_ge(s_tevA.h, NH * NEV)
            g.wait_ge(s_pso.h, NH * NQ)
            wall16(g, s_doS, NH)
            wall16(g, s_dwq, NH)
            for fc in range(NFC):
                if fc >= 2:
                    g.wait_ge(s_gmm.h, fc - 1)
                g.dma_start(out=Wg_b[fc % 2][:], in_=gw_d[fc]).then_inc(s_dgw[fc % 2].h, 16)
                if fc >= 2:
                    g.wait_ge(s_umm.h, fc - 1)
                g.dma_start(out=Wu_b[fc % 2][:], in_=uw_d[fc]).then_inc(s_duw[fc % 2].h, 16)

        sc35 = nc.enter_named_scope('p35_gateup', False)

        @block.tensor
        def _(t):
            for fc in range(NFC):
                w16(t, s_dgw, fc)
                if fc == 0:
                    t.wait_ge(s_exp.h, NH * NSEG)
                    t.wait_ge(s_msk.h, NH * NT)
                    t.wait_ge(s_tevD.h, NH * NEV)
                    t.wait_ge(s_tevA.h, NH * NEV)
                    t.wait_ge(s_pso.h, NH * NQ)
                else:
                    t.wait_ge(s_silu.h, fc)
                for dc in range(NDC):
                    for m in range(4):
                        ins = t.matmul(
                            ps_big[:, m * 512:(m + 1) * 512],
                            Wg_b[fc % 2][:, dc * 128:(dc + 1) * 128],
                            xnT[:, dc * T + m * 512:dc * T + (m + 1) * 512],
                            start=(dc == 0), stop=(dc == NDC - 1))
                ins.then_inc(s_gmm.h, 1)
                w16(t, s_duw, fc)
                if fc >= 1:
                    t.wait_ge(s_hm.h, fc)
                for dc in range(NDC):
                    for m in range(4):
                        ins = t.matmul(
                            ps_up[:, m * 512:(m + 1) * 512],
                            Wu_b[fc % 2][:, dc * 128:(dc + 1) * 128],
                            xnT[:, dc * T + m * 512:dc * T + (m + 1) * 512],
                            start=(dc == 0), stop=(dc == NDC - 1))
                ins.then_inc(s_umm.h, 1)

        @block.scalar
        def _(sc):
            sc.wait_ge(s_dsc.h, 48)
            for fc in range(NFC):
                sc.wait_ge(s_gmm.h, fc + 1)
                if fc >= 1:
                    sc.wait_ge(s_hm.h, fc)
                sc.activation(out=silu_g[:], in_=ps_big[:], func=AF.Silu,
                              bias=0.0, scale=gs_s[:, fc:fc + 1]).then_inc(s_silu.h, 1)

        @block.vector
        def _(v):
            for fc in range(NFC):
                v.wait_ge(s_umm.h, fc + 1)
                v.wait_ge(s_silu.h, fc + 1)
                if fc >= 2:
                    w16(v, s_dhS, fc - 2)
                v.scalar_tensor_tensor(
                    out=h_buf[fc % 2][:], in0=ps_up[:],
                    scalar=us_s[:, fc:fc + 1], in1=silu_g[:],
                    op0=ALU.mult, op1=ALU.mult).then_inc(s_hm.h, 1)

        @block.sync
        def _(s):
            for fc in range(NFC):
                s.wait_ge(s_hm.h, fc + 1)
                s.dma_start(out=h_dram[fc * 128:(fc + 1) * 128, :],
                            in_=h_buf[fc % 2][:]).then_inc(s_dhS[fc % 2].h, 16)

        nc.leave_named_scope('p35_gateup', sc35[0], False)

        for cm in reversed(p35_cms):
            cm.__exit__(None, None, None)
        xnT_cm.__exit__(None, None, None)
        p4_cms = [nc.sbuf_tensor("oT_all", [128, NH * T], BF16, side="right"),
                  nc.sbuf_tensor("wo_b0", [128, D], BF16, side="right"),
                  nc.sbuf_tensor("wo_b1", [128, D], BF16, side="right"),
                  nc.sbuf_tensor("at_b0", [128, T], BF16, side="right"),
                  nc.sbuf_tensor("at_b1", [128, T], BF16, side="right")]
        oT_all, wo_b0, wo_b1, at_b0, at_b1 = [cm.__enter__() for cm in p4_cms]
        wo_b = [wo_b0, wo_b1]
        at_b = [at_b0, at_b1]

        @block.gpsimd
        def _(g):
            g.wait_ge(s_umm.h, NFC)
            g.wait_ge(s_hm.h, NFC)
            wall16(g, s_dhS, NFC)
            wall16(g, s_dgw, NFC)
            wall16(g, s_duw, NFC)
            for do in range(NDC):
                if do >= 2:
                    g.wait_ge(s_omm.h, do - 1)
                g.dma_start(out=wo_b[do % 2][:], in_=wo_d[do]).then_inc(s_dwo[do % 2].h, 16)

        sc4 = nc.enter_named_scope('p4_oproj', False)

        @block.sync
        def _(s):
            s.wait_ge(s_umm.h, NFC)
            s.wait_ge(s_hm.h, NFC)
            wall16(s, s_dhS, NFC)
            wall16(s, s_dgw, NFC)
            wall16(s, s_duw, NFC)
            for di in range(0, NH, 2):
                s.dma_start(out=oT_all[:, di * T:(di + 1) * T],
                            in_=oT_dram[di]).then_inc(s_dolS.h, 16)

        @block.scalar
        def _(sc):
            sc.wait_ge(s_umm.h, NFC)
            sc.wait_ge(s_hm.h, NFC)
            wall16(sc, s_dhS, NFC)
            wall16(sc, s_dgw, NFC)
            wall16(sc, s_duw, NFC)
            for di in range(1, NH, 2):
                sc.dma_start(out=oT_all[:, di * T:(di + 1) * T],
                             in_=oT_dram[di]).then_inc(s_dolC.h, 16)

        @block.tensor
        def _(t):
            for do in range(NDC):
                psA = ps_big if do % 2 == 0 else ps_up
                w16(t, s_dwo, do)
                if do == 0:
                    t.wait_ge(s_silu.h, NFC)
                    t.wait_ge(s_hm.h, NFC)
                    t.wait_ge(s_dolS.h, 16 * 16)
                    t.wait_ge(s_dolC.h, 16 * 16)
                if do >= 2:
                    t.wait_ge(s_oev.h, do - 1)
                for di in range(NH):
                    for m in range(4):
                        ins = t.matmul(
                            psA[:, m * 512:(m + 1) * 512],
                            wo_b[do % 2][:, di * 128:(di + 1) * 128],
                            oT_all[:, di * T + m * 512:di * T + (m + 1) * 512],
                            start=(di == 0), stop=(di == NH - 1))
                ins.then_inc(s_omm.h, 1)

        @block.vector
        def _(v):
            for do in range(NDC):
                psA = ps_big if do % 2 == 0 else ps_up
                v.wait_ge(s_omm.h, do + 1)
                if do >= 2:
                    w16(v, s_daS, do - 2)
                v.tensor_copy(at_b[do % 2][:], psA[:]).then_inc(s_oev.h, 1)

        @block.sync
        def _(s):
            for do in range(NDC):
                s.wait_ge(s_oev.h, do + 1)
                s.dma_start(out=attnT_dram[do * 128:(do + 1) * 128, :],
                            in_=at_b[do % 2][:]).then_inc(s_daS[do % 2].h, 16)

        nc.leave_named_scope('p4_oproj', sc4[0], False)

        for cm in reversed(p4_cms):
            cm.__exit__(None, None, None)
        NPMAX = max(np_ for _f, np_ in PARTS)
        p5_cms = [nc.sbuf_tensor("h_part", [128, NPMAX * T], BF16, side="right"),
                  nc.sbuf_tensor("Wd_b0", [128, NPMAX * 128], BF16, side="right"),
                  nc.sbuf_tensor("Wd_b1", [128, NPMAX * 128], BF16, side="right"),
                  nc.sbuf_tensor("dp_in0", [128, T], F32, side="right"),
                  nc.sbuf_tensor("dp_in1", [128, T], F32, side="right"),
                  nc.sbuf_tensor("dp_out0", [128, T], F32, side="right"),
                  nc.sbuf_tensor("dp_out1", [128, T], F32, side="right"),
                  nc.sbuf_tensor("dn_b0", [128, T], BF16, side="right"),
                  nc.sbuf_tensor("dn_b1", [128, T], BF16, side="right")]
        (h_part, Wd_b0, Wd_b1, dp_in0, dp_in1,
         dp_out0, dp_out1, dn_b0, dn_b1) = [cm.__enter__() for cm in p5_cms]
        Wd_b = [Wd_b0, Wd_b1]
        dp_in = [dp_in0, dp_in1]
        dp_out = [dp_out0, dp_out1]
        dn_b = [dn_b0, dn_b1]

        hl_base = {}
        _c = 0
        for p, (fc0, npart) in enumerate(PARTS):
            hl_base[p] = _c
            _c += npart

        @block.gpsimd
        def _(g):
            g.wait_ge(s_omm.h, NDC)
            g.wait_ge(s_oev.h, NDC)
            wall16(g, s_daS, NDC)
            wall16(g, s_dwo, NDC)
            g.wait_ge(s_dolS.h, 16 * 16)
            g.wait_ge(s_dolC.h, 16 * 16)
            for p, (fc0, npart) in enumerate(PARTS):
                for dm in range(NDC):
                    if p * 32 + dm >= 2:
                        g.wait_ge(s_dmm.h, p * 32 + dm - 1)
                    g.dma_start(out=Wd_b[dm % 2][:, 0:npart * 128],
                                in_=dw_d[dm][:, fc0 * 128:(fc0 + npart) * 128]).then_inc(s_ddw[(p * 32 + dm) % 2].h, 16)

        sc5 = nc.enter_named_scope('p5_down', False)

        @block.sync
        def _(s):
            for p, (fc0, npart) in enumerate(PARTS):
                if p == 0:
                    s.wait_ge(s_omm.h, NDC)
                    s.wait_ge(s_oev.h, NDC)
                    wall16(s, s_daS, NDC)
                    wall16(s, s_dhS, NFC)
                    wall16(s, s_dwo, NDC)
                    s.wait_ge(s_dolS.h, 16 * 16)
                    s.wait_ge(s_dolC.h, 16 * 16)
                else:
                    s.wait_ge(s_dmm.h, p * 32)
                for j in range(npart):
                    fc = fc0 + j
                    s.dma_start(out=h_part[:, j * T:(j + 1) * T],
                                in_=h_dram[fc * 128:(fc + 1) * 128, :]).then_inc(s_dhL.h, 16)
                if p < 2:
                    for dm in range(NDC):
                        s.wait_ge(s_dev.h, dev_F(p, dm))
                        s.dma_start(out=dpart_dram[dm * 128:(dm + 1) * 128, :],
                                    in_=dp_out[dm % 2][:]).then_inc(s_dpS[(p * 32 + dm) % 2].h, 16)
            for dm in range(NDC):
                s.wait_ge(s_dev.h, dev_F(2, dm))
                s.dma_start(out=downT_dram[dm * 128:(dm + 1) * 128, :],
                            in_=dn_b[dm % 2][:]).then_inc(s_dnS[dm % 2].h, 16)

        @block.tensor
        def _(t):
            for p, (fc0, npart) in enumerate(PARTS):
                for dm in range(NDC):
                    psA = ps_big if dm % 2 == 0 else ps_up
                    w16(t, s_ddw, p * 32 + dm)
                    if dm == 0:
                        t.wait_ge(s_dhL.h, 16 * (hl_base[p] + npart))
                    if dm >= 2:
                        t.wait_ge(s_dev.h, dev_F(p, dm - 2))
                    elif p >= 1:
                        t.wait_ge(s_dev.h, dev_F(p - 1, NDC - 2 + dm))
                    for j in range(npart):
                        for m in range(4):
                            ins = t.matmul(
                                psA[:, m * 512:(m + 1) * 512],
                                Wd_b[dm % 2][:, j * 128:(j + 1) * 128],
                                h_part[:, j * T + m * 512:j * T + (m + 1) * 512],
                                start=(j == 0), stop=(j == npart - 1))
                    ins.then_inc(s_dmm.h, 1)

        @block.scalar
        def _(sc):
            for p in (1, 2):
                for dm in range(NDC):
                    w16(sc, s_dpS, (p - 1) * 32 + dm)
                    if dm >= 2:
                        sc.wait_ge(s_dev.h, dev_F(p, dm - 2))
                    elif p == 2:
                        sc.wait_ge(s_dev.h, dev_F(1, NDC - 2 + dm))
                    sc.dma_start(out=dp_in[dm % 2][:],
                                 in_=dpart_dram[dm * 128:(dm + 1) * 128, :]).then_inc(s_dpL[((p - 1) * 32 + dm) % 2].h, 16)

        @block.vector
        def _(v):
            for p, (fc0, npart) in enumerate(PARTS):
                for dm in range(NDC):
                    psA = ps_big if dm % 2 == 0 else ps_up
                    v.wait_ge(s_dmm.h, p * 32 + dm + 1)
                    if p == 0:
                        if dm >= 2:
                            w16(v, s_dpS, dm - 2)
                        v.tensor_copy(dp_out[dm % 2][:], psA[:]).then_inc(s_dev.h, 1)
                    elif p == 1:
                        w16(v, s_dpL, dm)
                        if dm >= 2:
                            w16(v, s_dpS, 32 + dm - 2)
                        else:
                            w16(v, s_dpS, NDC - 2 + dm)
                        v.tensor_tensor(out=dp_out[dm % 2][:], in0=psA[:],
                                        in1=dp_in[dm % 2][:], op=ALU.add).then_inc(s_dev.h, 1)
                    else:
                        w16(v, s_dpL, 32 + dm)
                        if dm >= 2:
                            w16(v, s_dnS, dm - 2)
                            wall16(v, s_dpS, 64)
                        else:
                            w16(v, s_dpS, 32 + NDC - 2 + dm)
                        v.tensor_tensor(out=dp_out[dm % 2][:], in0=psA[:],
                                        in1=dp_in[dm % 2][:], op=ALU.add).then_inc(s_dev.h, 1)
                        v.wait_ge(s_dev.h, 64 + 2 * dm + 1)
                        v.tensor_scalar(out=dn_b[dm % 2][:], in0=dp_out[dm % 2][:],
                                        scalar1=ds_s[:, dm:dm + 1], scalar2=None,
                                        op0=ALU.mult).then_inc(s_dev.h, 1)

        nc.leave_named_scope('p5_down', sc5[0], False)

        for cm in reversed(p5_cms):
            cm.__exit__(None, None, None)
        p6_cms = [nc.sbuf_tensor("x6_0", [128, D], F32, side="right"),
                  nc.sbuf_tensor("x6_1", [128, D], F32, side="right"),
                  nc.sbuf_tensor("a6_0", [128, D], BF16, side="right"),
                  nc.sbuf_tensor("a6_1", [128, D], BF16, side="right"),
                  nc.sbuf_tensor("d6_0", [128, D], BF16, side="right"),
                  nc.sbuf_tensor("d6_1", [128, D], BF16, side="right"),
                  nc.sbuf_tensor("o6_0", [128, D], F32, side="right"),
                  nc.sbuf_tensor("o6_1", [128, D], F32, side="right")]
        x6_0, x6_1, a6_0, a6_1, d6_0, d6_1, o6_0, o6_1 = [cm.__enter__() for cm in p6_cms]
        x6 = [x6_0, x6_1]
        a6 = [a6_0, a6_1]
        d6 = [d6_0, d6_1]
        o6 = [o6_0, o6_1]

        sc6 = nc.enter_named_scope('p6_final', False)

        @block.sync
        def _(s):
            s.wait_ge(s_dmm.h, 96)
            s.wait_ge(s_dev.h, dev_F(2, NDC - 1))
            wall16(s, s_dnS, NDC)
            wall16(s, s_dpS, 64)
            wall16(s, s_ddw, 96)
            s.wait_ge(s_dhL.h, 16 * 86)
            wall16(s, s_dpL, 64)
            def out_store(j):
                s.wait_ge(s_sum.h, 2 * (j + 1))
                s.dma_start(out=out_d[j * 128:(j + 1) * 128, :],
                            in_=o6[j % 2][:]).then_inc(s_dout[j % 2].h, 16)
            for i in range(NT):
                if i >= 2:
                    out_store(i - 2)
                s.dma_start_transpose(a6[i % 2][:],
                                      attnT_dram[:, i * 128:(i + 1) * 128]).then_inc(s_d6[i % 2].h, 16)
                s.dma_start_transpose(d6[i % 2][:],
                                      downT_dram[:, i * 128:(i + 1) * 128]).then_inc(s_d6[i % 2].h, 16)
            out_store(NT - 2)
            out_store(NT - 1)
            wall16(s, s_dout, NT)

        @block.scalar
        def _(sc):
            sc.wait_ge(s_dmm.h, 96)
            sc.wait_ge(s_dev.h, dev_F(2, NDC - 1))
            wall16(sc, s_dnS, NDC)
            wall16(sc, s_ddw, 96)
            sc.wait_ge(s_dhL.h, 16 * 86)
            wall16(sc, s_dpL, 64)
            for i in range(NT):
                if i >= 2:
                    sc.wait_ge(s_sum.h, 2 * (i - 1))
                sc.dma_start(out=x6[i % 2][:],
                             in_=x_d[i * 128:(i + 1) * 128, :]).then_inc(s_dx6[i % 2].h, 16)

        @block.vector
        def _(v):
            for i in range(NT):
                v.wait_ge(s_d6[i % 2].h, 32 * (i // 2 + 1))
                w16(v, s_dx6, i)
                if i >= 2:
                    w16(v, s_dout, i - 2)
                v.tensor_tensor(out=o6[i % 2][:], in0=a6[i % 2][:],
                                in1=d6[i % 2][:], op=ALU.add).then_inc(s_sum.h, 1)
                v.wait_ge(s_sum.h, 2 * i + 1)
                v.tensor_tensor(out=o6[i % 2][:], in0=o6[i % 2][:],
                                in1=x6[i % 2][:], op=ALU.add).then_inc(s_sum.h, 1)

        nc.leave_named_scope('p6_final', sc6[0], False)

        for cm in reversed(p6_cms):
            cm.__exit__(None, None, None)
        ps_up_cm.__exit__(None, None, None)

    return nc


def _host_layout(wq, wk, wv, wo, gw, uw, dw, gs, us, ds):
    wq_p = np.ascontiguousarray(
        wq.reshape(NDC, 128, NH, 128).transpose(2, 1, 0, 3).reshape(NH, 128, D))
    wo_p = np.ascontiguousarray(
        wo.reshape(NDC, 128, NDC, 128).transpose(2, 1, 0, 3).reshape(NDC, 128, D))
    wk_p = np.ascontiguousarray(
        wk.reshape(NDC, 128, 128).transpose(1, 0, 2).reshape(128, D))
    wv_p = np.ascontiguousarray(
        wv.reshape(NDC, 128, 128).transpose(1, 0, 2).reshape(128, D))
    gw_p = np.ascontiguousarray(
        gw.reshape(NFC, 128, NDC, 128).transpose(0, 3, 2, 1).reshape(NFC, 128, D))
    uw_p = np.ascontiguousarray(
        uw.reshape(NFC, 128, NDC, 128).transpose(0, 3, 2, 1).reshape(NFC, 128, D))
    dw_p = np.ascontiguousarray(
        dw.reshape(NDC, 128, NFC, 128).transpose(0, 3, 2, 1).reshape(NDC, 128, DFF))
    gs_p = np.ascontiguousarray(gs.reshape(NFC, 128).T)
    us_p = np.ascontiguousarray(us.reshape(NFC, 128).T)
    ds_p = np.ascontiguousarray(ds.reshape(NDC, 128).T)
    return wq_p, wk_p, wv_p, wo_p, gw_p, uw_p, dw_p, gs_p, us_p, ds_p


def kernel(L0_attn_x, L0_attn_wq, L0_attn_wk, L0_attn_wv, L0_attn_wo,
           L0_ffn_gate_w, L0_ffn_gate_s, L0_ffn_up_w, L0_ffn_up_s,
           L0_ffn_down_w, L0_ffn_down_s):
    x = np.ascontiguousarray(np.asarray(L0_attn_x, dtype=np.float32))
    (wq_p, wk_p, wv_p, wo_p, gw_p, uw_p, dw_p, gs_p, us_p, ds_p) = _host_layout(
        np.asarray(L0_attn_wq, np.float32), np.asarray(L0_attn_wk, np.float32),
        np.asarray(L0_attn_wv, np.float32), np.asarray(L0_attn_wo, np.float32),
        np.asarray(L0_ffn_gate_w, np.int32), np.asarray(L0_ffn_up_w, np.int32),
        np.asarray(L0_ffn_down_w, np.int32), np.asarray(L0_ffn_gate_s, np.float32),
        np.asarray(L0_ffn_up_s, np.float32), np.asarray(L0_ffn_down_s, np.float32))

    nc = build_nc()
    in_maps = []
    for c in range(8):
        in_maps.append(dict(
            x=np.ascontiguousarray(x[c]),
            wq_p=wq_p, wk_p=wk_p, wv_p=wv_p, wo_p=wo_p,
            gw_p=gw_p, uw_p=uw_p, dw_p=dw_p,
            gs_p=gs_p, us_p=us_p, ds_p=ds_p,
        ))
    res = run_bass_kernel_spmd(nc, in_maps, core_ids=list(range(8)))
    out = np.stack([res.results[c]["out"] for c in range(8)], axis=0)
    return out
